# version 01 (final)
# baseline (speedup 1.0000x reference)
import numpy as np
import ml_dtypes
import concourse.bass as bass
import concourse.mybir as mybir
from concourse.bass_utils import run_bass_kernel_spmd
from contextlib import ExitStack

F32 = mybir.dt.float32
BF16 = mybir.dt.bfloat16
I32 = mybir.dt.int32
OP = mybir.AluOpType
AF = mybir.ActivationFunctionType
AX = mybir.AxisListType

BF = ml_dtypes.bfloat16

NCORES = 8
T = 1024
NT = 8
DM = 2048
NF = 16
H = 16
DH = 128
INTER = 8192
NKI = 64
DIN = 4096
ND = 32
DN = 16
BL = 512
NJOBS = 17
MAGIC = 131072.0
LN_EPS = 1e-5
SCALE = DH ** -0.5

_LIMIT = "all"
_ORDER = ["ln1", "attn", "ffn1", "ln3", "mamba", "ffn2", "all", "ffA", "mbA"]
_DEBUG_TAP = None
_MB_STAGE = 9


def _want(ph):
    if _LIMIT in ("ffA", "mbA"):
        return False
    return _ORDER.index(ph) <= _ORDER.index(_LIMIT)


ENGS = ("sync", "scalar", "vector", "tensor", "gpsimd")
_REGC = [0]


def _rn(p):
    _REGC[0] += 1
    return f"{p}{_REGC[0]}"


class Prog:
    def __init__(self, nc, es):
        self.nc = nc
        self.es = es
        self.q = {e: [] for e in ENGS}
        self.sems = {}
        self.cnt = {}

    def sem(self, name):
        s = self.es.enter_context(self.nc.semaphore(name))
        self.sems[name] = s
        self.cnt[name] = 0
        return s

    def bump(self, name, inc):
        self.cnt[name] += inc
        return self.cnt[name]

    def emit(self, eng, fn):
        self.q[eng].append(fn)

    def dma(self, eng, out, in_, sem=None, **kw):
        name = sem or ("gp" if eng == "gpsimd" else "sy")
        v = self.bump(name, 16)
        s = self.sems[name]

        def f(e, out=out, in_=in_, s=s, kw=kw):
            e.dma_start(out=out, in_=in_, **kw).then_inc(s, 16)
        self.emit(eng, f)
        return v

    def op(self, eng, fn, sem=None, inc=1):
        if sem is None:
            self.emit(eng, fn)
            return None
        v = self.bump(sem, inc)
        s = self.sems[sem]
        if eng in ("vector", "scalar"):
            def f(e, fn=fn, s=s, inc=inc):
                fn(e)
                e.drain(fusable=False).then_inc(s, inc)
        else:
            def f(e, fn=fn, s=s, inc=inc):
                fn(e).then_inc(s, inc)
        self.emit(eng, f)
        return v

    def dr(self, eng):
        self.emit(eng, lambda e: e.drain(fusable=False))

    def wait(self, eng, sem, v):
        if v <= 0:
            return
        s = self.sems[sem]
        self.emit(eng, lambda e, s=s, v=v: e.wait_ge(s, v))

    def barrier(self, engs=ENGS):
        names = [n for n in ("sy", "gp", "te", "ve", "se") if n in self.sems]
        snap = [(n, self.cnt[n]) for n in names if self.cnt[n] > 0]
        for e in engs:
            for n, v in snap:
                self.wait(e, n, v)

    def run(self):
        block = self.es.enter_context(self.nc.Block())
        for ename in ENGS:
            ops = self.q[ename]

            def body(eng, ops=ops):
                for f in ops:
                    f(eng)
            getattr(block, ename)(body)


def _strips(w, tile_cols=128):
    K, M = w.shape
    nk = K // 128
    ne = M // tile_cols
    r = w.reshape(nk, 128, ne, tile_cols).transpose(2, 1, 0, 3).reshape(ne, 128, nk * tile_cols)
    return np.ascontiguousarray(r)


def _dcol(v):
    return np.ascontiguousarray(v.reshape(ND, 128).T)


def _host_prep(inputs):
    g = {}
    f = np.float32
    x = np.asarray(inputs["L0_attn_a_x"], f).reshape(8192, DM)
    wq = np.asarray(inputs["L0_attn_a_Wq"], f)
    wkv = np.asarray(inputs["L0_attn_a_Wkv"], f)
    wout = np.asarray(inputs["L0_attn_a_Wout"], f)
    win = np.asarray(inputs["L0_attn_b_in_proj_weight"], f)
    wconv = np.asarray(inputs["L0_attn_b_conv1d_weight"], f)
    wxp = np.asarray(inputs["L0_attn_b_x_proj_weight"], f)
    wdt = np.asarray(inputs["L0_attn_b_dt_proj_weight"], f)
    bdt = np.asarray(inputs["L0_attn_b_dt_proj_bias"], f)
    alog = np.asarray(inputs["L0_attn_b_A_log"], f)
    dpar = np.asarray(inputs["L0_attn_b_D"], f)
    wop = np.asarray(inputs["L0_attn_b_out_proj_weight"], f)
    wgu = np.asarray(inputs["L0_ffn_w_gate_up"], f)
    wdn = np.asarray(inputs["L0_ffn_w_down"], f)

    g["wq_s"] = _strips(wq)
    g["wkv_k"] = _strips(wkv[:, 0:128])[0]
    g["wkv_v"] = _strips(np.ascontiguousarray(wkv[:, 128:256]))[0]
    g["wout_t"] = np.ascontiguousarray(wout.reshape(H, 128, DM))
    g["win_s"] = _strips(np.ascontiguousarray(win.T))
    g["wgu_s"] = _strips(np.ascontiguousarray(wgu.T))
    g["wdn_t"] = np.ascontiguousarray(np.ascontiguousarray(wdn.T).reshape(NKI, 128, DM))
    wxpT = np.ascontiguousarray(wxp.T)
    g["wxp_dt"] = _strips(np.ascontiguousarray(wxpT[:, 0:128]))[0]
    g["wxp_bc"] = _strips(np.ascontiguousarray(wxpT[:, 128:160]), 32)[0]
    g["wdt_t"] = np.ascontiguousarray(wdt.T)
    g["wop_t"] = np.ascontiguousarray(np.ascontiguousarray(wop.T).reshape(ND, 128, DM))
    g["alog_t"] = np.ascontiguousarray(
        alog.reshape(ND, 128, DN).transpose(1, 0, 2).reshape(128, ND * DN))
    g["d_col"] = _dcol(dpar)
    g["bdt_col"] = _dcol(bdt)
    g["wconv_t"] = np.ascontiguousarray(
        wconv.reshape(ND, 128, 4).transpose(1, 0, 2).reshape(128, ND * 4))

    g["identb"] = np.eye(128).astype(BF)
    masks = np.zeros((128, 4 * BL), np.float32)
    kk = np.arange(128)[:, None]
    qq = np.arange(BL)[None, :]
    for st in range(4):
        masks[:, st * BL:(st + 1) * BL] = (128 * st + kk <= qq)
    g["masks"] = masks.astype(BF)

    cores = []
    for i in range(NCORES):
        gA, gB = i, 15 - i
        c = {}
        c["x_loc"] = np.ascontiguousarray(
            np.concatenate([x[gA * BL:(gA + 1) * BL], x[gB * BL:(gB + 1) * BL]], 0))
        pos = np.concatenate([np.arange(gA * BL, (gA + 1) * BL),
                              np.arange(gB * BL, (gB + 1) * BL)]).astype(np.float64)
        inv = 1.0 / 10000.0 ** (np.arange(0, DH, 2, dtype=np.float64) / DH)
        fr = inv[:, None] * pos[None, :]
        cs, sn = np.cos(fr), np.sin(fr)
        cos_t = np.concatenate([cs, cs], 0)
        sin_eff = np.concatenate([-sn, sn], 0)
        c["rope"] = np.ascontiguousarray(np.concatenate(
            [cos_t * SCALE, sin_eff * SCALE, cos_t, sin_eff], 1)).astype(BF)
        pp = np.arange(128, dtype=np.int64)
        kvidx = np.zeros((128, 48), np.int64)

        def rows(gb, kind):
            if gb is None:
                return (0 * 128 + pp) * 6 + 4
            r, s = (gb, 0) if gb < 8 else (15 - gb, 1)
            return (128 * r + pp) * 6 + (0 if kind == "k" else 2) + s
        slots = [gA, gB]
        slots += [j if j < gA else None for j in range(7)]
        slots += [j if j < gB else None for j in range(15)]
        for s, gb in enumerate(slots):
            kvidx[:, s] = rows(gb, "k")
            kvidx[:, 24 + s] = rows(gb, "v")
        c["kvidx"] = kvidx.astype(np.uint32)
        c["dncorr"] = np.array([[512.0 * (7 - gA), 512.0 * (15 - gB)]], np.float32)
        hidx = np.zeros((128, 2), np.int64)
        for blk, gcur in enumerate((gA, gB)):
            gp = gcur - 1
            if gp < 0:
                hidx[:, blk] = pp * 4 + 2
            else:
                r, s = (gp, 0) if gp < 8 else (15 - gp, 1)
                hidx[:, blk] = (128 * r + pp) * 4 + s
        c["haloidx"] = hidx.astype(np.uint32)
        sel = np.zeros((128, 32), np.float32)
        sel[:, gA] = 1.0
        sel[:, 16 + gB] = 1.0
        c["selab"] = sel.astype(BF)
        cores.append(c)
    return g, cores


def _owner(gb):
    return (gb, 0) if gb < 8 else (15 - gb, 1)


def _build():
    nc = bass.Bass("TRN2", target_bir_lowering=False)
    es = ExitStack()
    L = {"nc": nc}

    def din(name, shape, dt):
        L[name] = nc.dram_tensor(name, shape, dt, kind="ExternalInput").ap()

    din("x_loc", [T, DM], F32)
    din("rope", [128, 4096], BF16)
    din("kvidx", [128, 48], mybir.dt.uint32)
    din("haloidx", [128, 2], mybir.dt.uint32)
    din("selab", [128, 32], BF16)
    din("dncorr", [1, 2], F32)
    din("wq_s", [H, 128, DM], F32)
    din("wkv_k", [128, DM], F32)
    din("wkv_v", [128, DM], F32)
    din("wout_t", [H, 128, DM], F32)
    din("win_s", [64, 128, DM], F32)
    din("wgu_s", [128, 128, DM], F32)
    din("wdn_t", [NKI, 128, DM], F32)
    din("wxp_dt", [128, ND * 128], F32)
    din("wxp_bc", [128, ND * 32], F32)
    din("wdt_t", [128, DIN], F32)
    din("wop_t", [ND, 128, DM], F32)
    din("alog_t", [128, ND * DN], F32)
    din("d_col", [128, ND], F32)
    din("bdt_col", [128, ND], F32)
    din("wconv_t", [128, ND * 4], F32)
    din("identb", [128, 128], BF16)
    din("masks", [128, 4 * BL], BF16)

    L["out_d"] = nc.dram_tensor("xout", [T, DM], F32, kind="ExternalOutput").ap()
    if _DEBUG_TAP is not None:
        L["dbg_d"] = nc.dram_tensor("dbg", list(_DEBUG_TAP[1]), F32,
                                    kind="ExternalOutput").ap()

    L["ag1in"] = nc.dram_tensor("ag1in", [128, 3072], BF16)
    L["ag1out"] = nc.dram_tensor("ag1out", [8 * 128, 3072], BF16)
    L["ag2in"] = nc.dram_tensor("ag2in", [128, 384], BF16)
    L["ag2out"] = nc.dram_tensor("ag2out", [8 * 128, 384], BF16)
    L["ag3in"] = nc.dram_tensor("ag3in", [128, 1090], F32)
    L["ag3out"] = nc.dram_tensor("ag3out", [8 * 128, 1090], F32)
    L["rcpd"] = nc.dram_tensor("rcpd", [1, T], F32)
    L["zd"] = nc.dram_tensor("zd", [128, ND * T], BF16)
    L["y1d"] = nc.dram_tensor("y1d", [128, ND * T], F32)
    L["bctd"] = nc.dram_tensor("bctd", [32, T], BF16)
    L["wgu_c"] = nc.dram_tensor("wgu_c", [128, 128 * DM], BF16)
    L["wdn_c"] = nc.dram_tensor("wdn_c", [128, NKI * DM], BF16)

    def sb(name, shape, dt):
        L[name] = es.enter_context(nc.sbuf_tensor(name, shape, dt))

    sb("x_sb", [128, NT * DM], F32)
    sb("xhT", [128, NF * T], BF16)
    sb("arena", [128, 32 * T], BF16)
    sb("wbuf", [128, 2 * DM], BF16)
    sb("wkA", [128, 2560], F32)
    sb("wkB", [128, DM], F32)
    sb("wkC", [128, 3072], BF16)
    sb("identb_sb", [128, 128], BF16)
    sb("zeros_b", [128, 128], BF16)
    sb("cst", [128, 2], F32)
    sb("magic_b", [1, 2], BF16)
    sb("ones_row", [128, 1], BF16)
    sb("red", [128, 1], F32)
    sb("nmean", [128, 1], F32)
    sb("ssq", [128, 1], F32)
    sb("rstd", [128, 1], F32)
    sb("bias2", [128, 1], F32)
    sb("t_ln", [128, 1], F32)
    sb("A_sb", [128, ND * DN], F32)
    sb("dcol_sb", [128, ND], F32)
    sb("bdt_sb", [128, ND], F32)
    sb("conv_sb", [128, ND * 4], F32)
    sb("mst", [128, 3072], F32)
    sb("badi", [1, 1], I32)
    L["sent"] = L["mst"][0:1, 2048:2064]
    L["eqt"] = L["mst"][0:1, 2064:2080]
    L["eq2"] = L["mst"][0:1, 2080:2096]
    L["badf"] = L["mst"][0:1, 2096:2097]
    L["magic_f"] = L["mst"][0:1, 2560:2562]
    L["dnc_sb"] = L["mst"][0:1, 2562:2564]
    sb("kvx_sb", [128, 48], mybir.dt.uint32)
    sb("hix_sb", [128, 2], mybir.dt.uint32)
    sb("sel_sb", [128, 32], BF16)


    def ps(name, shape, dt):
        L[name] = es.enter_context(nc.psum_tensor(name, shape, dt))

    ps("p0", [128, 512], F32)
    ps("p1", [128, 512], F32)
    ps("p2", [128, 512], F32)
    ps("pAV", [128, 1024], F32)
    ps("pCS", [1, 1024], F32)
    ps("pT", [128, 1024], BF16)
    L["prot"] = [L["p0"], L["p1"], L["p2"]]

    P = Prog(nc, es)
    for n in ("sy", "gp", "te", "ve", "se", "cc1", "cc2", "cc3",
              "ck1", "ck2", "ck3", "rd1", "rd2", "rd3"):
        P.sem(n)
    L["regs"] = {}

    def prime(eng_name):
        def f(e, eng_name=eng_name):
            L["regs"][eng_name] = [e.alloc_register(f"{eng_name}_r{i}")
                                   for i in range(4)]
        return f
    for en in ENGS:
        P.emit(en, prime(en))

    emit_all(P, L)
    P.run()
    es.close()
    return nc


def kernel(**inputs):
    g, cores = _host_prep(inputs)
    nc = _build()
    in_maps = []
    for i in range(NCORES):
        m = dict(g)
        m.update(cores[i])
        in_maps.append(m)
    res = run_bass_kernel_spmd(nc, in_maps, core_ids=list(range(NCORES)))
    out = np.zeros((8192, DM), np.float32)
    for i in range(NCORES):
        xo = np.asarray(res.results[i]["xout"])
        gA, gB = i, 15 - i
        out[gA * BL:(gA + 1) * BL] = xo[0:BL]
        out[gB * BL:(gB + 1) * BL] = xo[BL:T]
    if _DEBUG_TAP is not None:
        kernel._dbg = [np.asarray(r["dbg"]) for r in res.results]
    return out.reshape(1, 8192, DM)


def emit_all(P, L):
    emit_loads(P, L)
    if _LIMIT == "ffA":
        emit_ln(P, L)
        emit_ffn(P, L, first=True)
        emit_store(P, L)
        return
    if _LIMIT == "mbA":
        emit_ln(P, L)
        emit_mamba(P, L)
        emit_store(P, L)
        return
    emit_ln(P, L)
    if _want("attn"):
        emit_attn(P, L)
    if _want("ffn1"):
        emit_ln(P, L)
        emit_ffn(P, L, first=True)
    if _want("ln3"):
        emit_ln(P, L)
    if _want("mamba"):
        emit_mamba(P, L)
    if _want("ffn2"):
        emit_ln(P, L)
        emit_ffn(P, L, first=False)
    emit_store(P, L)


def emit_loads(P, L):
    x_ap = bass.AP(L["x_loc"].tensor, 0, [[DM, 128], [128 * DM, NT], [1, DM]])
    P.dma("sync", L["x_sb"][:].rearrange("p (tt f) -> p tt f", f=DM), x_ap)
    P.dma("sync", L["identb_sb"][:], L["identb"][:, :])
    P.dma("sync", L["kvx_sb"][:], L["kvidx"][:, :])
    P.dma("sync", L["hix_sb"][:], L["haloidx"][:, :])
    P.dma("sync", L["sel_sb"][:], L["selab"][:, :])
    P.dma("sync", L["dnc_sb"][0:1, 0:2], L["dncorr"][:, :])
    P.dma("sync", L["dcol_sb"][:], L["d_col"][:, :])
    P.dma("sync", L["bdt_sb"][:], L["bdt_col"][:, :])
    P.dma("sync", L["conv_sb"][:], L["wconv_t"][:, :])
    v = P.dma("sync", L["A_sb"][:], L["alog_t"][:, :])
    P.wait("scalar", "sy", v)
    P.op("scalar", lambda e: e.activation(out=L["A_sb"][:], in_=L["A_sb"][:],
                                          func=AF.Exp), sem="se")
    P.op("scalar", lambda e: e.mul(out=L["A_sb"][:], in_=L["A_sb"][:], mul=-1.0), sem="se")
    P.op("vector", lambda e: e.memset(L["zeros_b"][:], 0.0), sem="ve")
    P.op("vector", lambda e: e.memset(L["magic_f"][0:1, 0:2], MAGIC), sem="ve")
    P.op("vector", lambda e: e.memset(L["cst"][:, 0:1], LN_EPS), sem="ve")
    P.op("vector", lambda e: e.memset(L["cst"][:, 1:2], 1.0), sem="ve")
    P.op("vector", lambda e: e.memset(L["magic_b"][:], MAGIC), sem="ve")
    P.op("vector", lambda e: e.memset(L["ones_row"][:], 1.0), sem="ve")
    zv = P.cnt["ve"]
    P.wait("sync", "ve", zv)
    P.op("vector", lambda e: e.memset(L["mst"][0:1, 0:2], 0.0), sem="ve")
    zv = P.cnt["ve"]
    P.wait("sync", "ve", zv)
    for agname, ncols, c1, c2, zt in (
            ("ag1out", 3072, 2560, 3071, None),
            ("ag2out", 384, 288, 383, None),
            ("ag3out", 1090, 0, 1089, "f")):
        ag = L[agname]
        zsrc = L["mst"][0:1, 0:2] if zt == "f" else L["zeros_b"][0:1, 0:2]
        for r in range(8):
            P.dma("sync", ag.ap()[128 * r:128 * r + 1, c1:c1 + 1], zsrc[0:1, 0:1])
            P.dma("sync", ag.ap()[128 * r + 127:128 * r + 128, c2:c2 + 1],
                  zsrc[0:1, 1:2])
    P.barrier()


def emit_ln(P, L):
    x_sb, xhT, pT = L["x_sb"], L["xhT"], L["pT"]
    scr = L["wkA"]
    sq = L["wkB"]
    xh_tok = L["wkC"]
    for tt in range(NT):
        xt = x_sb[:, DM * tt:DM * (tt + 1)]
        P.wait("vector", "se", P.cnt["se"])
        vv = P.op("vector", lambda e, xt=xt: e.reduce_sum(
            L["red"][:, 0:1], xt, axis=AX.X), sem="ve")
        P.wait("scalar", "ve", vv)
        P.wait("scalar", "te", P.cnt["te"])
        P.op("scalar", lambda e: e.mul(
            out=L["nmean"][:], in_=L["red"][:, 0:1], mul=-1.0 / DM), sem=None)
        P.dr("scalar")
        P.op("scalar", lambda e, xt=xt: e.add(
            out=scr[:, 0:DM], in_=xt, add=L["nmean"][:]), sem=None)
        sv = P.op("scalar", lambda e: e.activation(
            out=sq[:, 0:DM], in_=scr[:, 0:DM], func=AF.Square), sem="se")
        P.wait("vector", "se", sv)
        vv = P.op("vector", lambda e: e.reduce_sum(
            L["ssq"][:], sq[:, 0:DM], axis=AX.X), sem="ve")
        P.wait("scalar", "ve", vv)
        P.op("scalar", lambda e: e.activation(
            out=L["t_ln"][:], in_=L["ssq"][:], func=AF.Ln, scale=1.0 / DM,
            bias=L["cst"][:, 0:1]), sem=None)
        P.dr("scalar")
        P.op("scalar", lambda e: e.activation(
            out=L["rstd"][:], in_=L["t_ln"][:], func=AF.Exp, scale=-0.5), sem=None)
        P.dr("scalar")
        sv = P.op("scalar", lambda e: e.activation(
            out=xh_tok[:, 0:DM], in_=scr[:, 0:DM], func=AF.Identity,
            scale=L["rstd"][:]), sem="se")
        P.wait("tensor", "se", sv)
        for f in range(NF):
            P.wait("tensor", "ve", P.cnt["ve"])
            tv = P.op("tensor", lambda e, f=f: e.transpose(
                pT[:, 128 * (f % 8):128 * (f % 8) + 128],
                xh_tok[:, 128 * f:128 * f + 128], L["identb_sb"][:]), sem="te")
            P.wait("vector", "te", tv)
            P.op("vector", lambda e, f=f, tt=tt: e.tensor_copy(
                xhT[:, T * f + 128 * tt:T * f + 128 * tt + 128],
                pT[:, 128 * (f % 8):128 * (f % 8) + 128]), sem="ve")
    P.barrier()


def emit_store(P, L):
    P.barrier()
    x_ap = bass.AP(L["out_d"].tensor, 0, [[DM, 128], [128 * DM, NT], [1, DM]])
    v = P.dma("sync", x_ap, L["x_sb"][:].rearrange("p (tt f) -> p tt f", f=DM))
    P.wait("sync", "sy", v)
    if _DEBUG_TAP is not None:
        name, shp = _DEBUG_TAP
        v = P.dma("gpsimd", L["dbg_d"][:, :], L[name][0:shp[0], 0:shp[1]])
        P.wait("gpsimd", "gp", v)


def emit_spin_ag(P, L, agin, agout, ncols, idx, srcwait, soff1=None, soff2=None):
    if soff1 is None:
        soff1 = 0
    if soff2 is None:
        soff2 = 128 * ncols - 1
    P.wait("gpsimd", srcwait[0], srcwait[1])
    P.op("gpsimd", lambda e, agin=agin, agout=agout: e.collective_compute(
        "AllGather", OP.bypass, replica_groups=[list(range(8))],
        ins=[agin.ap().opt()], outs=[agout.ap().opt()]), sem="cc" + idx)
    P.wait("gpsimd", "cc" + idx, 1)
    ckname = "ck" + idx

    def spin(e, agout=agout, ncols=ncols, L=L, P=P, ckname=ckname, idx=idx,
             soff1=soff1, soff2=soff2):
        sct = L["sent"]
        if True:
            bad, exp = L["regs"]["gpsimd"][0], L["regs"]["gpsimd"][1]
            e.reg_mov(bad, 1)
            e.reg_mov(exp, 0)
            with e.While(bad):
                e.reg_add(exp, exp, 32)
                src1 = bass.AP(agout, soff1, [[1, 1], [128 * ncols, 8]])
                src2 = bass.AP(agout, soff2, [[1, 1], [128 * ncols, 8]])
                e.dma_start(out=sct[0:1, 0:8], in_=src1,
                            allow_slow_non_contiguous=True).then_inc(P.sems[ckname], 16)
                e.dma_start(out=sct[0:1, 8:16], in_=src2,
                            allow_slow_non_contiguous=True).then_inc(P.sems[ckname], 16)
                e.wait_ge(P.sems[ckname], exp)
                e.tensor_single_scalar(L["eqt"][0:1, 0:16], sct[0:1, 0:16], float(MAGIC), OP.is_equal)
                e.tensor_tensor(L["eq2"][:, 0:8], L["eqt"][:, 0:8], L["eqt"][:, 8:16], OP.add)
                e.tensor_tensor(L["eq2"][:, 8:12], L["eq2"][:, 0:4], L["eq2"][:, 4:8], OP.add)
                e.tensor_tensor(L["eq2"][:, 12:14], L["eq2"][:, 8:10], L["eq2"][:, 10:12], OP.add)
                e.tensor_tensor(L["eq2"][:, 14:15], L["eq2"][:, 12:13], L["eq2"][:, 13:14], OP.add)
                e.tensor_single_scalar(L["badf"][:], L["eq2"][:, 14:15], -16.0, OP.add)
                e.tensor_copy(L["badi"][:], L["badf"][0:1, 0:1])
                e.reg_load(bad, L["badi"][0:1, 0:1])
    P.emit("gpsimd", spin)
    P.op("gpsimd", lambda e: e.engine_nop(), sem="rd" + idx)


KJ_OFF = 0
VJ_OFF = 12288
AVT_OFF = 24576
MASK_OFF = 26624
ROPEQ_OFF = 28672
PT_OFF = 30720
NSLOT = 24
QOFFS = [0, BL] + [0] * 7 + [BL] * 15


def emit_attn(P, L):
    arn, xhT, wkC, wkA = L["arena"], L["xhT"], L["wkC"], L["wkA"]
    pAV, pCS, prot = L["pAV"], L["pCS"], L["prot"]
    mst = L["mst"]
    P.dma("sync", arn[:, 0:2048], L["rope"][:, 2048:4096])
    P.dma("sync", arn[:, ROPEQ_OFF:ROPEQ_OFF + 2048], L["rope"][:, 0:2048])
    rv_ = P.dma("sync", arn[:, MASK_OFF:MASK_OFF + 4 * BL], L["masks"][:, :])
    P.wait("vector", "sy", rv_)
    P.wait("tensor", "sy", rv_)

    P.dma("gpsimd", L["wbuf"][:, 0:DM], L["wkv_k"][:, :])
    gv = P.dma("gpsimd", L["wbuf"][:, DM:2 * DM], L["wkv_v"][:, :])
    P.wait("tensor", "gp", gv)
    for kt in range(NF):
        for ch in range(2):
            P.op("tensor", lambda e, kt=kt, ch=ch: e.matmul(
                pAV[:, 512 * ch:512 * ch + 512], L["wbuf"][:, 128 * kt:128 * kt + 128],
                xhT[:, T * kt + 512 * ch:T * kt + 512 * ch + 512],
                start=(kt == 0), stop=(kt == NF - 1)), sem="te")
    tv = P.cnt["te"]
    P.wait("vector", "te", tv)
    P.op("vector", lambda e: e.tensor_mul(wkA[:, 0:1024], pAV[:, :],
                                          arn[:, 0:1024]), sem="ve")
    P.op("vector", lambda e: e.tensor_mul(wkA[0:64, 1024:2048], pAV[64:128, :],
                                          arn[0:64, 1024:2048]), sem="ve")
    P.op("vector", lambda e: e.tensor_mul(wkA[64:128, 1024:2048], pAV[0:64, :],
                                          arn[64:128, 1024:2048]), sem="ve")
    P.op("vector", lambda e: e.tensor_add(wkC[:, 0:1024], wkA[:, 0:1024],
                                          wkA[:, 1024:2048]), sem="ve")
    vev = []
    for tt in range(NT):
        pp = L["p0"] if tt % 2 == 0 else L["p1"]
        if len(vev) >= 2:
            P.wait("tensor", "ve", vev[-2])
        for kt in range(NF):
            P.op("tensor", lambda e, tt=tt, kt=kt, pp=pp: e.matmul(
                pp[:, 0:128],
                xhT[:, T * kt + 128 * tt:T * kt + 128 * tt + 128],
                L["wbuf"][:, DM + 128 * kt:DM + 128 * kt + 128],
                start=(kt == 0), stop=(kt == NF - 1)), sem="te")
        tv = P.cnt["te"]
        P.wait("vector", "te", tv)
        vv = P.op("vector", lambda e, tt=tt, pp=pp: e.tensor_copy(
            wkC[:, 1024 + 128 * tt:1024 + 128 * tt + 128], pp[:, 0:128]), sem="ve")
        vev.append(vv)
    vv = P.cnt["ve"]
    P.wait("sync", "ve", vv)
    P.dma("sync", L["ag1in"].ap()[:, 0:1024], wkC[:, 0:1024])
    P.dma("sync", L["ag1in"].ap()[:, 1024:2048], wkC[:, 1024:2048])
    for z in range(4):
        P.dma("sync", L["ag1in"].ap()[:, 2048 + 128 * z:2048 + 128 * z + 128],
              L["zeros_b"][:, 0:128])
    P.dma("sync", L["ag1in"].ap()[0:1, 2560:2561], L["magic_b"][0:1, 0:1])
    sv = P.dma("sync", L["ag1in"].ap()[127:128, 3071:3072], L["magic_b"][0:1, 1:2])
    emit_spin_ag(P, L, L["ag1in"], L["ag1out"], 3072, "1", ("sy", sv), soff1=2560)

    P.wait("gpsimd", "rd1", 1)
    P.wait("gpsimd", "ve", P.cnt["ve"])
    def gathers(e, L=L, P=P):
        agv = bass.AP(L["ag1out"], 0, [[512, 6144], [1, 512]])
        for s in range(NSLOT):
            e.indirect_dma_start(
                out=L["arena"][:, KJ_OFF + 512 * s:KJ_OFF + 512 * s + 512],
                out_offset=None, in_=agv,
                in_offset=bass.IndirectOffsetOnAxis(L["kvx_sb"][:, s:s + 1], 0),
            ).then_inc(P.sems["gp"], 16)
        for s in range(NSLOT):
            e.indirect_dma_start(
                out=L["arena"][:, VJ_OFF + 512 * s:VJ_OFF + 512 * s + 512],
                out_offset=None, in_=agv,
                in_offset=bass.IndirectOffsetOnAxis(L["kvx_sb"][:, 24 + s:25 + s], 0),
            ).then_inc(P.sems["gp"], 16)
    P.emit("gpsimd", gathers)
    kvv = P.bump("gp", 48 * 16)
    P.wait("tensor", "gp", kvv)
    P.wait("scalar", "gp", kvv)

    hist = {"se": [], "av": [], "vm": []}
    for h in range(H):
        P.wait("gpsimd", "te", P.cnt["te"])
        gv = P.dma("gpsimd", L["wbuf"][:, 0:DM], L["wq_s"][h, :, :])
        P.wait("tensor", "gp", gv)
        P.wait("tensor", "ve", P.cnt["ve"])
        for kt in range(NF):
            for ch in range(2):
                P.op("tensor", lambda e, kt=kt, ch=ch: e.matmul(
                    pAV[:, 512 * ch:512 * ch + 512],
                    L["wbuf"][:, 128 * kt:128 * kt + 128],
                    xhT[:, T * kt + 512 * ch:T * kt + 512 * ch + 512],
                    start=(kt == 0), stop=(kt == NF - 1)), sem="te")
        tv = P.cnt["te"]
        P.wait("vector", "te", tv)
        P.op("vector", lambda e: e.tensor_mul(
            wkA[:, 0:1024], pAV[:, :], arn[:, ROPEQ_OFF:ROPEQ_OFF + 1024]), sem="ve")
        P.op("vector", lambda e: e.tensor_mul(
            wkA[0:64, 1024:2048], pAV[64:128, :],
            arn[0:64, ROPEQ_OFF + 1024:ROPEQ_OFF + 2048]), sem="ve")
        P.op("vector", lambda e: e.tensor_mul(
            wkA[64:128, 1024:2048], pAV[0:64, :],
            arn[64:128, ROPEQ_OFF + 1024:ROPEQ_OFF + 2048]), sem="ve")
        qv = P.op("vector", lambda e: e.tensor_add(
            wkC[:, 2048:3072], wkA[:, 0:1024], wkA[:, 1024:2048]), sem="ve")
        P.wait("tensor", "ve", qv)
        for ch in range(2):
            P.op("tensor", lambda e, ch=ch: e.matmul(
                pAV[:, 512 * ch:512 * ch + 512], L["zeros_b"][:, 0:128],
                L["xhT"][:, 0:512], start=True, stop=True), sem="te")
            P.op("tensor", lambda e, ch=ch: e.matmul(
                pCS[0:1, 512 * ch:512 * ch + 512], L["zeros_b"][:, 0:1],
                L["xhT"][:, 0:512], start=True, stop=True), sem="te")
        emit_attn_jobs(P, L, h, hist)

        tv = P.cnt["te"]
        P.wait("vector", "te", tv)
        P.op("vector", lambda e: e.tensor_single_scalar(
            mst[0:1, 0:512], pCS[0:1, 0:512], L["dnc_sb"][0:1, 0:1],
            OP.subtract), sem="ve")
        P.op("vector", lambda e: e.tensor_single_scalar(
            mst[0:1, 512:1024], pCS[0:1, 512:1024], L["dnc_sb"][0:1, 1:2],
            OP.subtract), sem="ve")
        P.op("vector", lambda e: e.reciprocal(mst[0:1, 1024:1536],
                                              mst[0:1, 0:512]), sem="ve")
        rv = P.op("vector", lambda e: e.reciprocal(mst[0:1, 1536:2048],
                                                   mst[0:1, 512:1024]), sem="ve")
        P.wait("sync", "ve", rv)
        P.dma("sync", L["rcpd"].ap()[0:1, :], mst[0:1, 1024:2048])
        sv = P.dma("sync", wkA[:, 0:1024],
                   L["rcpd"].ap()[0:1, :].broadcast_to([128, T]))
        P.wait("vector", "sy", sv)
        av_slot = AVT_OFF + 1024 * (h % 2)
        P.op("vector", lambda e, av_slot=av_slot: e.tensor_mul(
            arn[:, av_slot:av_slot + 1024], pAV[:, :], wkA[:, 0:1024]), sem="ve")

        if h % 2 == 1:
            vv = P.cnt["ve"]
            P.wait("gpsimd", "ve", vv)
            P.wait("gpsimd", "te", P.cnt["te"])
            P.dma("gpsimd", L["wbuf"][:, 0:DM], L["wout_t"][h - 1, :, :])
            wv = P.dma("gpsimd", L["wbuf"][:, DM:2 * DM], L["wout_t"][h, :, :])
            P.wait("tensor", "gp", wv)
            P.wait("tensor", "ve", vv)
            opv = []
            for tt in range(NT):
                for ncx in range(4):
                    k = tt * 4 + ncx
                    pp = prot[k % 3]
                    if k >= 3:
                        P.wait("tensor", "ve", opv[k - 3])
                    for gg in range(2):
                        P.op("tensor", lambda e, gg=gg, tt=tt, ncx=ncx, pp=pp: e.matmul(
                            pp[:],
                            arn[:, AVT_OFF + 1024 * gg + 128 * tt:
                                AVT_OFF + 1024 * gg + 128 * tt + 128],
                            L["wbuf"][:, DM * gg + 512 * ncx:DM * gg + 512 * ncx + 512],
                            start=(gg == 0), stop=(gg == 1)), sem="te")
                    tvx = P.cnt["te"]
                    P.wait("vector", "te", tvx)
                    opv.append(P.op("vector", lambda e, tt=tt, ncx=ncx, pp=pp: e.tensor_add(
                        L["x_sb"][:, DM * tt + 512 * ncx:DM * tt + 512 * ncx + 512],
                        L["x_sb"][:, DM * tt + 512 * ncx:DM * tt + 512 * ncx + 512],
                        pp[:]), sem="ve"))
    P.barrier()


def emit_attn_jobs(P, L, h, hist):
    arnp = L["arena"]
    se_h, av_h, vm_h = hist["se"], hist["av"], hist["vm"]

    for j in range(NSLOT):
        masked = j < 2
        qoff = QOFFS[j]
        base = len(se_h)
        for st in range(4):
            se_h.append(P.cnt["se"] + st + 1)

        for st in range(4):
            gi = base + st
            if gi >= 3 and se_h[gi - 3] > 0:
                P.wait("tensor", "se", se_h[gi - 3])
            rot = gi % 3
            P.op("tensor", lambda e, j=j, st=st, rot=rot, qoff=qoff: e.matmul(
                L["prot"][rot][:],
                arnp[:, KJ_OFF + 512 * j + 128 * st:KJ_OFF + 512 * j + 128 * st + 128],
                L["wkC"][:, 2048 + qoff:2048 + qoff + 512],
                start=True, stop=True), sem="te")
            sc_end = P.cnt["te"]

            P.wait("scalar", "te", sc_end)
            if gi >= 3:
                gp = gi - 3
                if vm_h[gp] > 0:
                    P.wait("scalar", "ve", vm_h[gp])
                elif gp < len(av_h):
                    P.wait("scalar", "te", av_h[gp])
            P.op("scalar", lambda e, rot=rot: e.activation(
                out=arnp[:, PT_OFF + BL * rot:PT_OFF + BL * rot + BL],
                in_=L["prot"][rot][:], func=AF.Exp), sem="se")
            assert se_h[gi] == P.cnt["se"]

            if masked:
                P.wait("vector", "se", se_h[gi])
                if gi >= 1 and len(av_h) > gi - 1:
                    P.wait("vector", "te", av_h[gi - 1])
                P.op("vector", lambda e, rot=rot, st=st: e.tensor_mul(
                    arnp[:, PT_OFF + BL * 3:PT_OFF + BL * 3 + BL],
                    arnp[:, PT_OFF + BL * rot:PT_OFF + BL * rot + BL],
                    arnp[:, MASK_OFF + BL * st:MASK_OFF + BL * st + BL]), sem="ve")
                vm_h.append(P.cnt["ve"])
            else:
                vm_h.append(0)

            pslot = 3 if masked else rot
            if vm_h[gi] > 0:
                P.wait("tensor", "ve", vm_h[gi])
            else:
                P.wait("tensor", "se", se_h[gi])
            pts = arnp[:, PT_OFF + BL * pslot:PT_OFF + BL * pslot + BL]
            P.op("tensor", lambda e, j=j, st=st, qoff=qoff, pts=pts: e.matmul(
                L["pAV"][:, qoff:qoff + 512],
                arnp[:, VJ_OFF + 512 * j + 128 * st:VJ_OFF + 512 * j + 128 * st + 128],
                pts, start=False, stop=False), sem="te")
            P.op("tensor", lambda e, qoff=qoff, pts=pts: e.matmul(
                L["pCS"][0:1, qoff:qoff + 512], L["ones_row"][:, 0:1], pts,
                start=False, stop=False), sem="te")
            av_h.append(P.cnt["te"])


def emit_ffn(P, L, first):
    arn, xhT, wkC = L["arena"], L["xhT"], L["wkC"]
    p0, p1, p2, pAV, prot = L["p0"], L["p1"], L["p2"], L["pAV"], L["prot"]
    ldq = "gpsimd" if first else "sync"
    ldsem = "gp" if first else "sy"
    dn_ve = []
    for grp in range(8):
        for kk in range(8):
            k = 8 * grp + kk
            slot = 2048 * kk
            P.wait(ldq, "te", P.cnt["te"])
            if first:
                v = P.dma("gpsimd", arn[:, slot:slot + DM], L["wdn_t"][k, :, :])
                P.wait("sync", "gp", v)
                P.dma("sync", L["wdn_c"].ap()[:, DM * k:DM * k + DM],
                      arn[:, slot:slot + DM])
            else:
                P.dma("sync", arn[:, slot:slot + DM],
                      L["wdn_c"].ap()[:, DM * k:DM * k + DM])
        wdn_v = P.cnt[ldsem]

        for kk in range(8):
            k = 8 * grp + kk
            P.wait(ldq, "te", P.cnt["te"])
            if first:
                v = P.dma("gpsimd", arn[:, 24576:24576 + DM], L["wgu_s"][k, :, :])
                P.wait("sync", "gp", v)
                P.dma("sync", L["wgu_c"].ap()[:, DM * k:DM * k + DM],
                      arn[:, 24576:24576 + DM])
            else:
                v = P.dma("sync", arn[:, 24576:24576 + DM],
                          L["wgu_c"].ap()[:, DM * k:DM * k + DM])
            P.wait("tensor", ldsem, v)
            P.wait("tensor", "se", P.cnt["se"])
            for kt in range(NF):
                for ch in range(2):
                    pp = p0 if ch == 0 else p1
                    P.op("tensor", lambda e, kt=kt, ch=ch, pp=pp: e.matmul(
                        pp[:], arn[:, 24576 + 128 * kt:24576 + 128 * kt + 128],
                        xhT[:, T * kt + 512 * ch:T * kt + 512 * ch + 512],
                        start=(kt == 0), stop=(kt == NF - 1)), sem="te")
            tv = P.cnt["te"]
            P.wait("scalar", "te", tv)
            P.wait("scalar", "ve", P.cnt["ve"])
            csl = 1024 * (kk % 2)
            P.op("scalar", lambda e, csl=csl: e.activation(
                out=wkC[:, csl:csl + 512], in_=p0[:], func=AF.Silu), sem="se")
            sv = P.op("scalar", lambda e, csl=csl: e.activation(
                out=wkC[:, csl + 512:csl + 1024], in_=p1[:], func=AF.Silu), sem="se")
            P.wait(ldq, "te", P.cnt["te"])
            if first:
                v = P.dma("gpsimd", arn[:, 26624:26624 + DM], L["wgu_s"][64 + k, :, :])
                P.wait("sync", "gp", v)
                P.dma("sync", L["wgu_c"].ap()[:, DM * (64 + k):DM * (64 + k) + DM],
                      arn[:, 26624:26624 + DM])
            else:
                v = P.dma("sync", arn[:, 26624:26624 + DM],
                          L["wgu_c"].ap()[:, DM * (64 + k):DM * (64 + k) + DM])
            P.wait("tensor", ldsem, v)
            P.wait("tensor", "ve", P.cnt["ve"])
            for kt in range(NF):
                for ch in range(2):
                    pp = p2 if ch == 0 else pAV
                    P.op("tensor", lambda e, kt=kt, ch=ch, pp=pp: e.matmul(
                        pp[:, 0:512], arn[:, 26624 + 128 * kt:26624 + 128 * kt + 128],
                        xhT[:, T * kt + 512 * ch:T * kt + 512 * ch + 512],
                        start=(kt == 0), stop=(kt == NF - 1)), sem="te")
            tv = P.cnt["te"]
            P.wait("vector", "te", tv)
            P.wait("vector", "se", sv)
            hslot = 16384 + 1024 * kk
            P.op("vector", lambda e, hslot=hslot, csl=csl: e.tensor_mul(
                arn[:, hslot:hslot + 512], wkC[:, csl:csl + 512], p2[:]), sem="ve")
            P.op("vector", lambda e, hslot=hslot, csl=csl: e.tensor_mul(
                arn[:, hslot + 512:hslot + 1024], wkC[:, csl + 512:csl + 1024],
                pAV[:, 0:512]), sem="ve")
        hm_v = P.cnt["ve"]

        P.wait("tensor", "ve", hm_v)
        P.wait("tensor", ldsem, wdn_v)
        for tt in range(NT):
            for ncx in range(4):
                kidx = tt * 4 + ncx
                pp = prot[kidx % 3]
                if len(dn_ve) >= 3:
                    P.wait("tensor", "ve", dn_ve[-3])
                for kk in range(8):
                    P.op("tensor", lambda e, kk=kk, tt=tt, ncx=ncx, pp=pp: e.matmul(
                        pp[:],
                        arn[:, 16384 + 1024 * kk + 128 * tt:
                            16384 + 1024 * kk + 128 * tt + 128],
                        arn[:, 2048 * kk + 512 * ncx:2048 * kk + 512 * ncx + 512],
                        start=(kk == 0), stop=(kk == 7)), sem="te")
                tvx = P.cnt["te"]
                P.wait("vector", "te", tvx)
                vv = P.op("vector", lambda e, tt=tt, ncx=ncx, pp=pp: e.tensor_add(
                    L["x_sb"][:, DM * tt + 512 * ncx:DM * tt + 512 * ncx + 512],
                    L["x_sb"][:, DM * tt + 512 * ncx:DM * tt + 512 * ncx + 512],
                    pp[:]), sem="ve")
                dn_ve.append(vv)
    P.barrier()


def emit_mamba(P, L):
    arn, xhT, wkA, wkB, wkC = (L["arena"], L["xhT"], L["wkA"], L["wkB"], L["wkC"])
    scr = wkA
    p0, p1, p2, pAV = L["p0"], L["p1"], L["p2"], L["pAV"]

    for e_i in range(64):
        slot = 2048 * (e_i % 2)
        P.wait("gpsimd", "te", P.cnt["te"])
        gv = P.dma("gpsimd", L["wbuf"][:, slot:slot + DM], L["win_s"][e_i, :, :])
        P.wait("tensor", "gp", gv)
        P.wait("tensor", "ve", P.cnt["ve"])
        P.wait("tensor", "se", P.cnt["se"])
        for kt in range(NF):
            for ch in range(2):
                pp = p0 if ch == 0 else p1
                P.op("tensor", lambda e, kt=kt, ch=ch, pp=pp, slot=slot: e.matmul(
                    pp[:], L["wbuf"][:, slot + 128 * kt:slot + 128 * kt + 128],
                    xhT[:, T * kt + 512 * ch:T * kt + 512 * ch + 512],
                    start=(kt == 0), stop=(kt == NF - 1)), sem="te")
        tv = P.cnt["te"]
        if e_i < 32:
            P.wait("vector", "te", tv)
            P.op("vector", lambda e, e_i=e_i: e.tensor_copy(
                arn[:, 1024 * e_i:1024 * e_i + 512], p0[:]), sem="ve")
            P.op("vector", lambda e, e_i=e_i: e.tensor_copy(
                arn[:, 1024 * e_i + 512:1024 * e_i + 1024], p1[:]), sem="ve")
        else:
            zsl = 1024 * (e_i % 2)
            P.wait("scalar", "te", tv)
            P.wait("scalar", "sy", P.cnt["sy"])
            P.op("scalar", lambda e, zsl=zsl: e.activation(
                out=wkC[:, zsl:zsl + 512], in_=p0[:], func=AF.Silu), sem="se")
            sv = P.op("scalar", lambda e, zsl=zsl: e.activation(
                out=wkC[:, zsl + 512:zsl + 1024], in_=p1[:], func=AF.Silu), sem="se")
            P.wait("sync", "se", sv)
            P.dma("sync", L["zd"].ap()[:, 1024 * (e_i - 32):1024 * (e_i - 32) + 1024],
                  wkC[:, zsl:zsl + 1024])

    if _MB_STAGE < 2:
        P.barrier()
        return
    P.wait("vector", "te", P.cnt["te"])
    P.op("vector", lambda e: e.tensor_copy(
        wkC[:, 2048:2144].rearrange("p (a b) -> p a b", b=3),
        bass.AP(arn, 509, [[32 * T, 128], [T, 32], [1, 3]])), sem="ve")
    vv = P.op("vector", lambda e: e.tensor_copy(
        wkC[:, 2144:2240].rearrange("p (a b) -> p a b", b=3),
        bass.AP(arn, 1021, [[32 * T, 128], [T, 32], [1, 3]])), sem="ve")
    P.wait("sync", "ve", vv)
    P.dma("sync", L["ag2in"].ap()[:, 0:96], wkC[:, 2048:2144])
    P.dma("sync", L["ag2in"].ap()[:, 96:192], wkC[:, 2144:2240])
    P.dma("sync", L["ag2in"].ap()[:, 192:288], L["zeros_b"][:, 0:96])
    P.dma("sync", L["ag2in"].ap()[0:1, 288:289], L["magic_b"][0:1, 0:1])
    sv = P.dma("sync", L["ag2in"].ap()[127:128, 383:384], L["magic_b"][0:1, 1:2])
    emit_spin_ag(P, L, L["ag2in"], L["ag2out"], 384, "2", ("sy", sv), soff1=288)
    P.wait("sync", "rd2", 1)

    def hload(e, L=L, P=P):
        agv2 = bass.AP(L["ag2out"], 0, [[96, 4096], [1, 96]])
        for blk in range(2):
            e.indirect_dma_start(
                out=L["wkC"][:, 2240 + 96 * blk:2240 + 96 * blk + 96],
                out_offset=None, in_=agv2,
                in_offset=bass.IndirectOffsetOnAxis(L["hix_sb"][:, blk:blk + 1], 0),
            ).then_inc(P.sems["gp"], 16)
    P.wait("gpsimd", "rd2", 1)
    P.emit("gpsimd", hload)
    hv = P.bump("gp", 32)
    P.wait("vector", "gp", hv)

    for e_i in range(ND):
        for blk in range(2):
            base = 1024 * e_i + 512 * blk
            halo = L["wkC"][:, 2240 + 96 * blk + 3 * e_i:2240 + 96 * blk + 3 * e_i + 3]
            sA, sB = scr[:, 0:512], scr[:, 512:1024]
            P.wait("vector", "se", P.cnt["se"])
            P.op("vector", lambda e, halo=halo, e_i=e_i: e.tensor_scalar(
                sA[:, 0:0 + 3] if False else scr[:, 0:3], halo[:, 0:3],
                L["conv_sb"][:, 4 * e_i:4 * e_i + 1], None, OP.mult), sem="ve")
            P.op("vector", lambda e, base=base, e_i=e_i: e.tensor_scalar(
                scr[:, 3:512], arn[:, base:base + 509],
                L["conv_sb"][:, 4 * e_i:4 * e_i + 1], None, OP.mult), sem="ve")
            for w in range(1, 4):
                src_dst = (scr[:, 0:512], scr[:, 512:1024]) if w % 2 == 1 else \
                    (scr[:, 512:1024], scr[:, 0:512])
                accin, accout = src_dst
                hw = 3 - w
                if hw > 0:
                    P.op("vector", lambda e, halo=halo, w=w, e_i=e_i, hw=hw,
                         accin=accin, accout=accout: e.scalar_tensor_tensor(
                        accout[:, 0:hw], halo[:, w:3],
                        L["conv_sb"][:, 4 * e_i + w:4 * e_i + w + 1],
                        accin[:, 0:hw], OP.mult, OP.add), sem="ve")
                P.op("vector", lambda e, base=base, w=w, e_i=e_i, hw=hw,
                     accin=accin, accout=accout: e.scalar_tensor_tensor(
                    accout[:, hw:512], arn[:, base:base + 512 - hw],
                    L["conv_sb"][:, 4 * e_i + w:4 * e_i + w + 1],
                    accin[:, hw:512], OP.mult, OP.add), sem="ve")
            vv = P.cnt["ve"]
            P.wait("scalar", "ve", vv)
            P.op("scalar", lambda e, base=base: e.activation(
                out=arn[:, base:base + 512], in_=scr[:, 512:1024],
                func=AF.Silu), sem="se")
    cv_se = P.cnt["se"]

    if _MB_STAGE < 3:
        P.barrier()
        return
    P.wait("gpsimd", "se", cv_se)
    gv2 = P.dma("gpsimd", L["wbuf"][:, 1024:2048], L["wxp_bc"][:, :])
    P.wait("tensor", "gp", gv2)
    P.wait("tensor", "se", cv_se)
    for blk in range(2):
        for half in range(2):
            P.wait("gpsimd", "te", P.cnt["te"])
            gv = P.dma("gpsimd", wkC[:, 0:2048],
                       L["wxp_dt"][:, 2048 * half:2048 * half + 2048])
            P.wait("tensor", "gp", gv)
            for dx in range(16):
                dd = 16 * half + dx
                P.op("tensor", lambda e, dx=dx, dd=dd, blk=blk: e.matmul(
                    p0[:], wkC[:, 128 * dx:128 * dx + 128],
                    arn[:, 1024 * dd + 512 * blk:1024 * dd + 512 * blk + 512],
                    start=(dd == 0), stop=(dd == ND - 1)), sem="te")
        for dd in range(ND):
            P.op("tensor", lambda e, dd=dd, blk=blk: e.matmul(
                p1[0:32, :], L["wbuf"][:, 1024 + 32 * dd:1024 + 32 * dd + 32],
                arn[:, 1024 * dd + 512 * blk:1024 * dd + 512 * blk + 512],
                start=(dd == 0), stop=(dd == ND - 1)), sem="te")
        tv = P.cnt["te"]
        P.wait("vector", "te", tv)
        P.op("vector", lambda e, blk=blk: e.tensor_copy(
            L["wbuf"][:, 512 * blk:512 * blk + 512], p0[:]), sem="ve")
        vv = P.op("vector", lambda e, blk=blk: e.tensor_copy(
            L["wbuf"][0:32, 2048 + 512 * blk:2048 + 512 * blk + 512],
            p1[0:32, :]), sem="ve")
        P.wait("tensor", "ve", vv)
    vv = P.cnt["ve"]
    P.wait("sync", "ve", vv)
    bv = P.dma("sync", L["bctd"].ap()[:, :], L["wbuf"][0:32, 2048:3072])
    P.wait("sync", "sy", bv)
    if _MB_STAGE < 4:
        P.barrier()
        return
    P.op("vector", lambda e: e.memset(L["mst"][:, 2048:2560], 0.0), sem="ve")
    P.wait("tensor", "ve", P.cnt["ve"])
    y1_sy = 0
    for blk in range(2):
        P.wait("sync", "te", P.cnt["te"])
        P.wait("sync", "ve", P.cnt["ve"])
        for n in range(DN):
            P.dma("sync", xhT[:, 1024 * n:1024 * n + 512],
                  bass.AP(L["bctd"], T * n + 512 * blk, [[0, 128], [1, 512]]))
            P.dma("sync", xhT[:, 1024 * n + 512:1024 * n + 1024],
                  bass.AP(L["bctd"], T * (DN + n) + 512 * blk, [[0, 128], [1, 512]]))
        bc_v = P.cnt["sy"]
        P.wait("vector", "sy", bc_v)
        for dd in range(ND):
            if dd % 16 == 0:
                P.wait("gpsimd", "te", P.cnt["te"])
                gv = P.dma("gpsimd", wkC[:, 0:2048],
                           L["wdt_t"][:, 2048 * (dd // 16):2048 * (dd // 16) + 2048])
                P.wait("tensor", "gp", gv)
            P.wait("tensor", "se", P.cnt["se"])
            P.wait("tensor", "ve", P.cnt["ve"])
            P.op("tensor", lambda e, dd=dd, blk=blk: e.matmul(
                p2[:], wkC[:, 128 * (dd % 16):128 * (dd % 16) + 128],
                L["wbuf"][:, 512 * blk:512 * blk + 512],
                start=True, stop=True), sem="te")
            tv = P.cnt["te"]
            P.wait("scalar", "te", tv)
            P.wait("scalar", "ve", P.cnt["ve"])
            P.op("scalar", lambda e, dd=dd: e.activation(
                out=scr[:, 1536:2048], in_=p2[:], func=AF.Abs,
                bias=L["bdt_sb"][:, dd:dd + 1]), sem="se")
            P.op("scalar", lambda e: e.activation(
                out=scr[:, 2048:2560], in_=scr[:, 1536:2048], func=AF.Exp,
                scale=-1.0), sem="se")
            sv = P.op("scalar", lambda e: e.activation(
                out=scr[:, 1536:2048], in_=scr[:, 2048:2560], func=AF.Ln,
                bias=L["cst"][:, 1:2]), sem="se")
            P.wait("vector", "te", tv)
            P.wait("vector", "se", sv)
            P.op("vector", lambda e, dd=dd: e.tensor_scalar(
                wkA[:, 2048:2560], p2[:], L["bdt_sb"][:, dd:dd + 1], 0.0,
                OP.add, OP.max), sem="ve")
            P.op("vector", lambda e: e.tensor_add(
                wkA[:, 0:512], wkA[:, 2048:2560], scr[:, 1536:2048]), sem="ve")
            P.op("vector", lambda e: e.tensor_tensor_scan(
                wkA[:, 512:1024], wkA[:, 0:512], L["mst"][:, 2048:2560], 0.0,
                OP.add, OP.add), sem="ve")
            P.op("vector", lambda e, dd=dd, blk=blk: e.tensor_mul(
                wkA[:, 1024:1536], wkA[:, 0:512],
                arn[:, 1024 * dd + 512 * blk:1024 * dd + 512 * blk + 512]), sem="ve")
            P.op("vector", lambda e, dd=dd, blk=blk: e.tensor_copy(
                L["mst"][:, 1024 + ND * blk + dd:1024 + ND * blk + dd + 1],
                wkA[:, 1023:1024]), sem="ve")
            dv = P.cnt["ve"]
            te_y = []
            for n in range(DN):
                dsl = 512 * (n % 2)
                P.wait("scalar", "ve", dv if n < 2 else P.cnt["ve"])
                sv = P.op("scalar", lambda e, dsl=dsl, dd=dd, n=n: e.activation(
                    out=wkB[:, dsl:dsl + 512], in_=wkA[:, 0:512], func=AF.Exp,
                    scale=L["A_sb"][:, DN * dd + n:DN * dd + n + 1]), sem="se")
                P.op("vector", lambda e, n=n: e.tensor_mul(
                    wkB[:, 1024:1536], wkA[:, 1024:1536],
                    xhT[:, 1024 * n:1024 * n + 512]), sem="ve")
                P.wait("vector", "se", sv)
                P.op("vector", lambda e, dsl=dsl: e.tensor_tensor_scan(
                    wkB[:, 1536:2048], wkB[:, dsl:dsl + 512], wkB[:, 1024:1536],
                    0.0, OP.mult, OP.add), sem="ve")
                P.op("vector", lambda e, blk=blk, dd=dd, n=n: e.tensor_copy(
                    L["mst"][:, 512 * blk + DN * dd + n:512 * blk + DN * dd + n + 1],
                    wkB[:, 2047:2048]), sem="ve")
                if len(te_y) >= 2:
                    P.wait("vector", "te", te_y[-2])
                pv = P.op("vector", lambda e, n=n: e.tensor_mul(
                    L["wbuf"][:, 1024 + 512 * (n % 2):1024 + 512 * (n % 2) + 512],
                    wkB[:, 1536:2048],
                    xhT[:, 1024 * n + 512:1024 * n + 1024]), sem="ve")
                P.wait("tensor", "ve", pv)
                tvy = P.op("tensor", lambda e, n=n: e.matmul(
                    pAV[:, 0:512], L["identb_sb"][:],
                    L["wbuf"][:, 1024 + 512 * (n % 2):1024 + 512 * (n % 2) + 512],
                    start=(n == 0), stop=(n == DN - 1)), sem="te")
                te_y.append(tvy)
            P.wait("vector", "te", te_y[-1])
            if y1_sy:
                P.wait("vector", "sy", y1_sy)
            yv = P.op("vector", lambda e, dd=dd, blk=blk: e.scalar_tensor_tensor(
                L["mst"][:, 1536:2048],
                arn[:, 1024 * dd + 512 * blk:1024 * dd + 512 * blk + 512],
                L["dcol_sb"][:, dd:dd + 1], pAV[:, 0:512], OP.mult, OP.add), sem="ve")
            P.wait("sync", "ve", yv)
            y1_sy = P.dma("sync", L["y1d"].ap()[:, 1024 * dd + 512 * blk:
                                                1024 * dd + 512 * blk + 512],
                          L["mst"][:, 1536:2048])
            P.wait("tensor", "ve", yv)

    if _MB_STAGE < 5:
        P.barrier()
        return
    vv = P.cnt["ve"]
    P.wait("sync", "ve", vv)
    P.dma("sync", L["ag3in"].ap()[0:1, 0:1], L["magic_f"][0:1, 0:1])
    P.dma("sync", L["ag3in"].ap()[127:128, 1089:1090], L["magic_f"][0:1, 1:2])
    P.dma("sync", L["ag3in"].ap()[:, 1:513], L["mst"][:, 0:512])
    P.dma("sync", L["ag3in"].ap()[:, 513:545], L["mst"][:, 1024:1056])
    P.dma("sync", L["ag3in"].ap()[:, 545:1057], L["mst"][:, 512:1024])
    sv = P.dma("sync", L["ag3in"].ap()[:, 1057:1089], L["mst"][:, 1056:1088])
    emit_spin_ag(P, L, L["ag3in"], L["ag3out"], 1090, "3", ("sy", sv))

    P.wait("vector", "rd3", 1)
    P.wait("sync", "rd3", 1)
    for z in range(5):
        P.op("vector", lambda e, z=z: e.memset(
            L["mst"][:, 512 * z:512 * z + 512], 0.0), sem="ve")
    for g in range(16):
        r, s = _owner(g)
        P.wait("sync", "ve", P.cnt["ve"])
        fv = P.dma("sync", L["wkA"][:, 0:544],
                   bass.AP(L["ag3out"], (128 * r) * 1090 + 1 + 544 * s,
                           [[1090, 128], [1, 544]]))
        cur = 2048 + 512 * (g % 2)
        nxt = 2048 + 512 * ((g + 1) % 2)
        P.op("vector", lambda e, g=g, cur=cur: e.scalar_tensor_tensor(
            L["mst"][:, 512 * ((g + 1) % 2):512 * ((g + 1) % 2) + 512],
            L["mst"][:, cur:cur + 512], L["sel_sb"][:, g:g + 1],
            L["mst"][:, 512 * (g % 2):512 * (g % 2) + 512],
            OP.mult, OP.add), sem="ve")
        P.op("vector", lambda e, g=g, cur=cur: e.scalar_tensor_tensor(
            L["mst"][:, 1024 + 512 * ((g + 1) % 2):1024 + 512 * ((g + 1) % 2) + 512],
            L["mst"][:, cur:cur + 512], L["sel_sb"][:, 16 + g:17 + g],
            L["mst"][:, 1024 + 512 * (g % 2):1024 + 512 * (g % 2) + 512],
            OP.mult, OP.add), sem="ve")
        P.wait("vector", "sy", fv)
        for dd in range(ND):
            P.op("vector", lambda e, dd=dd: e.tensor_scalar(
                wkB[:, DN * dd:DN * dd + DN], L["A_sb"][:, DN * dd:DN * dd + DN],
                L["wkA"][:, 512 + dd:513 + dd], None, OP.mult), sem=None)
        P.op("vector", lambda e: e.engine_nop(), sem="ve")
        vv = P.cnt["ve"]
        P.wait("scalar", "ve", vv)
        sv = P.op("scalar", lambda e: e.activation(
            out=wkB[:, 512:1024], in_=wkB[:, 0:512], func=AF.Exp), sem="se")
        P.wait("vector", "se", sv)
        P.op("vector", lambda e, cur=cur: e.tensor_mul(
            wkB[:, 1024:1536], wkB[:, 512:1024], L["mst"][:, cur:cur + 512]), sem=None)
        P.op("vector", lambda e, nxt=nxt: e.tensor_add(
            L["mst"][:, nxt:nxt + 512], wkB[:, 1024:1536],
            L["wkA"][:, 0:512]), sem="ve")

    if _MB_STAGE < 6:
        P.barrier()
        return
    P.wait("vector", "se", P.cnt["se"])
    P.op("vector", lambda e: e.memset(wkB[:, 1536:2048], 0.0), sem="ve")
    y1_sy = 0
    zs_sy = 0
    for blk in range(2):
        P.wait("sync", "te", P.cnt["te"])
        P.wait("sync", "ve", P.cnt["ve"])
        for n in range(DN):
            P.dma("sync", xhT[:, 1024 * n:1024 * n + 512],
                  bass.AP(L["bctd"], T * n + 512 * blk, [[0, 128], [1, 512]]))
            P.dma("sync", xhT[:, 1024 * n + 512:1024 * n + 1024],
                  bass.AP(L["bctd"], T * (DN + n) + 512 * blk, [[0, 128], [1, 512]]))
        bc_v = P.cnt["sy"]
        P.wait("vector", "sy", bc_v)
        for dd in range(ND):
            if dd % 16 == 0:
                P.wait("gpsimd", "te", P.cnt["te"])
                gv = P.dma("gpsimd", wkC[:, 0:2048],
                           L["wdt_t"][:, 2048 * (dd // 16):2048 * (dd // 16) + 2048])
                P.wait("tensor", "gp", gv)
            P.wait("tensor", "se", P.cnt["se"])
            P.wait("tensor", "ve", P.cnt["ve"])
            P.op("tensor", lambda e, dd=dd, blk=blk: e.matmul(
                p2[:], wkC[:, 128 * (dd % 16):128 * (dd % 16) + 128],
                L["wbuf"][:, 512 * blk:512 * blk + 512],
                start=True, stop=True), sem="te")
            tv = P.cnt["te"]
            P.wait("scalar", "te", tv)
            P.wait("scalar", "ve", P.cnt["ve"])
            P.op("scalar", lambda e, dd=dd: e.activation(
                out=scr[:, 1536:2048], in_=p2[:], func=AF.Abs,
                bias=L["bdt_sb"][:, dd:dd + 1]), sem="se")
            P.op("scalar", lambda e: e.activation(
                out=scr[:, 2048:2560], in_=scr[:, 1536:2048], func=AF.Exp,
                scale=-1.0), sem="se")
            sv = P.op("scalar", lambda e: e.activation(
                out=scr[:, 1536:2048], in_=scr[:, 2048:2560], func=AF.Ln,
                bias=L["cst"][:, 1:2]), sem="se")
            P.wait("vector", "te", tv)
            P.wait("vector", "se", sv)
            P.op("vector", lambda e, dd=dd: e.tensor_scalar(
                wkA[:, 2048:2560], p2[:], L["bdt_sb"][:, dd:dd + 1], 0.0,
                OP.add, OP.max), sem="ve")
            P.op("vector", lambda e: e.tensor_add(
                wkA[:, 0:512], wkA[:, 2048:2560], scr[:, 1536:2048]), sem="ve")
            dv = P.op("vector", lambda e: e.tensor_tensor_scan(
                wkA[:, 512:1024], wkA[:, 0:512], wkB[:, 1536:2048], 0.0,
                OP.add, OP.add), sem="ve")
            te_y = []
            for n in range(DN):
                dsl = 512 * (n % 2)
                P.wait("scalar", "ve", dv if n < 2 else P.cnt["ve"])
                sv = P.op("scalar", lambda e, dsl=dsl, dd=dd, n=n: e.activation(
                    out=wkB[:, dsl:dsl + 512], in_=wkA[:, 512:1024], func=AF.Exp,
                    scale=L["A_sb"][:, DN * dd + n:DN * dd + n + 1]), sem="se")
                P.wait("vector", "se", sv)
                if len(te_y) >= 2:
                    P.wait("vector", "te", te_y[-2])
                pv = P.op("vector", lambda e, dsl=dsl, n=n, blk=blk, dd=dd:
                          e.scalar_tensor_tensor(
                    L["wbuf"][:, 1024 + 512 * (n % 2):1024 + 512 * (n % 2) + 512],
                    wkB[:, dsl:dsl + 512],
                    L["mst"][:, 1024 * blk + DN * dd + n:1024 * blk + DN * dd + n + 1],
                    xhT[:, 1024 * n + 512:1024 * n + 1024],
                    OP.mult, OP.mult), sem="ve")
                P.wait("tensor", "ve", pv)
                tvy = P.op("tensor", lambda e, n=n: e.matmul(
                    pAV[:, 512:1024], L["identb_sb"][:],
                    L["wbuf"][:, 1024 + 512 * (n % 2):1024 + 512 * (n % 2) + 512],
                    start=(n == 0), stop=(n == DN - 1)), sem="te")
                te_y.append(tvy)
            if y1_sy:
                P.wait("sync", "ve", P.cnt["ve"])
            y1_sy = P.dma("sync", L["mst"][:, 512:1024],
                          L["y1d"].ap()[:, 1024 * dd + 512 * blk:
                                        1024 * dd + 512 * blk + 512])
            zs_sy = P.dma("sync", L["wbuf"][:, 2048:2560],
                          L["zd"].ap()[:, 1024 * dd + 512 * blk:
                                       1024 * dd + 512 * blk + 512])
            P.wait("vector", "te", te_y[-1])
            P.wait("vector", "sy", zs_sy)
            P.op("vector", lambda e: e.tensor_add(
                wkB[:, 1024:1536], L["mst"][:, 512:1024], pAV[:, 512:1024]), sem="ve")
            P.op("vector", lambda e, dd=dd, blk=blk: e.tensor_mul(
                arn[:, 1024 * dd + 512 * blk:1024 * dd + 512 * blk + 512],
                wkB[:, 1024:1536], L["wbuf"][:, 2048:2560]), sem="ve")

    if _MB_STAGE < 7:
        P.barrier()
        return
    op_ve = []
    for grp in range(4):
        P.wait("gpsimd", "ve", P.cnt["ve"])
        P.wait("gpsimd", "te", P.cnt["te"])
        for gg in range(8):
            dd = 8 * grp + gg
            gv = P.dma("gpsimd", xhT[:, 2048 * gg:2048 * gg + DM],
                       L["wop_t"][dd, :, :])
        P.wait("tensor", "gp", P.cnt["gp"])
        P.wait("tensor", "ve", P.cnt["ve"])
        for tt in range(NT):
            for ncx in range(4):
                kidx = tt * 4 + ncx
                pp = L["prot"][kidx % 3]
                if len(op_ve) >= 3:
                    P.wait("tensor", "ve", op_ve[-3])
                for gg in range(8):
                    dd = 8 * grp + gg
                    P.op("tensor", lambda e, gg=gg, dd=dd, tt=tt, ncx=ncx, pp=pp:
                         e.matmul(
                        pp[:], arn[:, 1024 * dd + 128 * tt:1024 * dd + 128 * tt + 128],
                        xhT[:, 2048 * gg + 512 * ncx:2048 * gg + 512 * ncx + 512],
                        start=(gg == 0), stop=(gg == 7)), sem="te")
                tvx = P.cnt["te"]
                P.wait("vector", "te", tvx)
                vv = P.op("vector", lambda e, tt=tt, ncx=ncx, pp=pp: e.tensor_add(
                    L["x_sb"][:, DM * tt + 512 * ncx:DM * tt + 512 * ncx + 512],
                    L["x_sb"][:, DM * tt + 512 * ncx:DM * tt + 512 * ncx + 512],
                    pp[:]), sem="ve")
                op_ve.append(vv)
    P.barrier()
